# version 35
# baseline (speedup 1.0000x reference)
import numpy as np
import concourse.bass as bass
import concourse.mybir as mybir
from concourse.bass_utils import run_bass_kernel_spmd

F32 = mybir.dt.float32
BF16 = mybir.dt.bfloat16
AF = mybir.ActivationFunctionType
ALU = mybir.AluOpType
AX = mybir.AxisListType

NCORES = 8
NTOK, DM, NH, HD, MF = 2048, 1024, 16, 64, 256
NTT, NSL = 16, 4
LN16 = 2.772588722239781
LNEPS, BNEPS, ZEPS = 1e-5, 1e-5, 1e-6
NALL = 16384.0

ENGS = ("sync", "gpsimd", "tensor", "scalar", "vector")
ENG_SEM = {"tensor": "T", "scalar": "S", "vector": "V", "gpsimd": "G", "sync": "Y"}

A_LN1T = lambda c: 2048 * c
A_W = lambda kc: 1024 * kc
A_LW = lambda br, T: 8192 + 2048 * br + 1024 * T
A_XG = lambda b: 12288 + 512 * b
A_XA = lambda br: 13312 + 256 * br
P_STA = 0
P_STB = 8192
P_KT = lambda mc: 512 * mc
P_V = lambda t4: 4096 + 1040 * t4
P_PK = lambda i: 8256 + 256 * (i % 4)
P_XS = lambda t4: 9280 + 1024 * t4
P_LNS = 13376
B_X2 = lambda c: 2048 * c
C_WO = lambda i: 1024 * i
C_WI = lambda i: 2048 + 1024 * i
C_PU = lambda br: 4096 + 1024 * br
C_SC = lambda br: 2048 * br
C_LB = lambda br: 6144 + 1024 * br
D_QT = lambda c: 512 * c
D_Q2 = 4096
D_PQ = lambda i: 5120 + 256 * (i % 2)
D_PQS = lambda i: 4608 + 512 * (i % 2)
D_XT = lambda t: 5632 + 1024 * (t % 2)
D_XCB = lambda br: 7680 + 256 * br
D_XG32 = lambda b: 512 * b
D_BIAS = 1024
D_X1 = lambda t: 2048 + 1024 * (t % 2)
D_TMP2 = 4096
D_XA32 = lambda br: 4608 + 256 * br
D_MI = lambda t: 5632 + 1024 * (t % 2)
D_OBT = 7680
E_KV = lambda mc: 1040 * mc
E_FB = lambda t, nch: 1024 * (t % 2) + 512 * nch
E_OM2 = 2080
E_S16 = lambda c: 2336 + 16 * c
E_ID = 2464
E_ONE = 2592
E_VEC = 2593
E_WC = 2605
E_NK = 3117
E_SQ = lambda t: 3181 + 16 * (t % 2)
E_ST = 3213
E_BO = 3221
E_BM = 3229
E_LC = 3237
E_ZC = 3249
E_BS = 3257
E_CST = 3265
E_NB = 3267
E_LC2 = 3271
E_LC4 = 3285
E_COLS = 3328
EB_WC = lambda c, br, lo: 512 * lo + 64 * c + 32 * br

GROUP_TOTAL_SEMS = ("WSETUP", "WLW", "WLB", "WGT")


class Ref:
    __slots__ = ("eng", "idx", "semov")

    def __init__(self, eng, idx, semov=None):
        self.eng, self.idx, self.semov = eng, idx, semov


class Builder:
    def __init__(self):
        self.streams = {e: [] for e in ENGS}
        self.dmacnt = {}
        self.cccnt = 0

    def op(self, eng, fn, deps=()):
        ref = Ref(eng, len(self.streams[eng]))
        self.streams[eng].append(
            {"fn": fn, "deps": [d for d in deps if d is not None], "semov": None})
        return ref

    def dma(self, eng, sem, out, in_, deps=()):
        self.dmacnt[sem] = self.dmacnt.get(sem, 0) + 16
        semov = (sem, self.dmacnt[sem], 16)
        ref = Ref(eng, len(self.streams[eng]), semov)
        self.streams[eng].append(
            {"fn": (lambda h, o=out, i=in_: h.dma_start(out=o, in_=i)),
             "deps": [d for d in deps if d is not None], "semov": semov})
        return ref

    def cc(self, fn, deps=()):
        self.cccnt += 1
        semov = ("CC", self.cccnt, 1)
        ref = Ref("gpsimd", len(self.streams["gpsimd"]), semov)
        self.streams["gpsimd"].append(
            {"fn": fn, "deps": [d for d in deps if d is not None], "semov": semov})
        return ref

    def finalize_marks(self):
        marked = set()
        for e in ENGS:
            for rec in self.streams[e]:
                for d in rec["deps"]:
                    if d.semov is None:
                        marked.add((d.eng, d.idx))
        self.marknum = {}
        for e in ENGS:
            k = 0
            for i, rec in enumerate(self.streams[e]):
                if rec["semov"] is None and (e, i) in marked:
                    k += 1
                    self.marknum[(e, i)] = k
        self.marked = marked

    def run_stream(self, eng, h, sems):
        lastw = {}
        for i, rec in enumerate(self.streams[eng]):
            for d in rec["deps"]:
                if d.semov is not None:
                    sem, val = d.semov[0], d.semov[1]
                    if sem in GROUP_TOTAL_SEMS:
                        val = self.dmacnt[sem]
                else:
                    sem, val = ENG_SEM[d.eng], self.marknum[(d.eng, d.idx)]
                if lastw.get(sem, -1) >= val:
                    continue
                h.wait_ge(sems[sem], val)
                lastw[sem] = val
            inst = rec["fn"](h)
            if rec["semov"] is not None:
                sem, _val, amt = rec["semov"]
                inst.then_inc(sems[sem], amt)
            elif (eng, i) in self.marked:
                inst.then_inc(sems[ENG_SEM[eng]], 1)


class Pool:
    def __init__(self, tensors):
        self.slots = [{"t": t, "free": None, "held": False, "slow": False}
                      for t in tensors]
        self.rr = 0

    def acquire(self, prefer_fast=False):
        n = len(self.slots)
        pick = None
        passes = (False, True) if prefer_fast else (None,)
        for want_slow in passes:
            for k in range(n):
                s = self.slots[(self.rr + k) % n]
                if not s["held"] and (want_slow is None
                                      or s["slow"] == want_slow):
                    pick = k
                    break
            if pick is not None:
                break
        if pick is None:
            raise RuntimeError("psum pool exhausted")
        s = self.slots[(self.rr + pick) % n]
        self.rr = (self.rr + pick + 1) % n
        s["held"] = True
        return s

    def release(self, s, consumer_ref):
        s["held"] = False
        s["free"] = consumer_ref
        s["slow"] = (consumer_ref is not None
                     and consumer_ref.eng in ("vector", "gpsimd"))


def _realize(nc, bld):
    bld.finalize_marks()
    import contextlib
    with contextlib.ExitStack() as es:
        sems = {}
        for name in ("T", "S", "V", "G", "Y", "CC", "WSETUP", "WGT", "WLW", "WLB", "WQ",
                     "XP1", "XT0", "XT1", "XW0", "XW1", "XR0", "XR1", "OUT",
                     "BNW", "BNR", "WST", "WSU"):
            sems[name] = es.enter_context(nc.semaphore(name))
        block = es.enter_context(nc.Block())

        @block.sync
        def _(h):
            bld.run_stream("sync", h, sems)

        @block.gpsimd
        def _(h):
            bld.run_stream("gpsimd", h, sems)

        @block.tensor
        def _(h):
            bld.run_stream("tensor", h, sems)

        @block.scalar
        def _(h):
            bld.run_stream("scalar", h, sems)

        @block.vector
        def _(h):
            bld.run_stream("vector", h, sems)


def _dump_and_finish(nc, bld, dumps):
    deps = []
    for e in ENGS:
        st = bld.streams[e]
        if st:
            deps.append(Ref(e, len(st) - 1, st[-1]["semov"]))
    for sem, tot in list(bld.dmacnt.items()):
        deps.append(Ref("sync", -1, (sem, tot, 16)))
    last = None
    for name, t, off, n in dumps:
        od = nc.dram_tensor("dbg_" + name, [128, n], t.dtype,
                            kind="ExternalOutput").ap()
        last = bld.dma("sync", "OUT", od, t[:, off:off + n], deps=deps)
    bld.op("sync", lambda h: h.nop(), deps=[last])
    _realize(nc, bld)


def build_program(stop=None):
    nc = bass.Bass("TRN2", target_bir_lowering=False, debug=False)

    xd = nc.dram_tensor("x", [NTOK, DM], F32, kind="ExternalInput").ap()
    wq = nc.dram_tensor("wq", [DM, DM], F32, kind="ExternalInput").ap()
    wk = nc.dram_tensor("wk", [DM, DM], F32, kind="ExternalInput").ap()
    wv = nc.dram_tensor("wv", [DM, DM], F32, kind="ExternalInput").ap()
    wgt = nc.dram_tensor("wgt", [DM, DM], F32, kind="ExternalInput").ap()
    om2 = nc.dram_tensor("om2", [128, MF], F32, kind="ExternalInput").ap()
    lwso = nc.dram_tensor("lwso", [MF, DM], F32, kind="ExternalInput").ap()
    lwsi = nc.dram_tensor("lwsi", [MF, DM], F32, kind="ExternalInput").ap()
    lbso = nc.dram_tensor("lbso", [128, DM], F32, kind="ExternalInput").ap()
    lbsi = nc.dram_tensor("lbsi", [128, DM], F32, kind="ExternalInput").ap()
    bbc = nc.dram_tensor("bbc", [128, DM], F32, kind="ExternalInput").ap()
    wcd = nc.dram_tensor("wc", [DM, 64], F32, kind="ExternalInput").ap()
    vec = nc.dram_tensor("vecs", [128, 12], F32, kind="ExternalInput").ap()
    s16d = nc.dram_tensor("s16", [128, 128], F32, kind="ExternalInput").ap()
    idd = nc.dram_tensor("ident", [128, 128], F32, kind="ExternalInput").ap()
    oned = nc.dram_tensor("ones", [128, 1], F32, kind="ExternalInput").ap()
    cstd = nc.dram_tensor("csts", [128, 2], F32, kind="ExternalInput").ap()
    outd = nc.dram_tensor("out", [NTOK, DM], F32, kind="ExternalOutput").ap()
    x1d = nc.dram_tensor("x1d", [NTOK, DM], F32).ap()
    bni = nc.dram_tensor("bn_in", [128, 8], F32)
    bno = nc.dram_tensor("bn_out", [128, 8], F32)
    dmy_i = nc.dram_tensor("dmy_in", [128, 8], F32)
    dmy_o = nc.dram_tensor("dmy_out", [128, 8], F32)

    Ah = nc.alloc_sbuf_tensor_at("Ah", [128, 16384], BF16, offset=16640)
    Al = nc.alloc_sbuf_tensor_at("Al", [128, 16384], BF16, offset=49408)
    Ps = nc.alloc_sbuf_tensor_at("Pscr", [128, 16384], F32, offset=82176)
    Bh = nc.alloc_sbuf_tensor_at("Bh", [128, 16384], BF16, offset=82176)
    Bl = nc.alloc_sbuf_tensor_at("Bl", [128, 16384], BF16, offset=114944)
    Ch = nc.alloc_sbuf_tensor_at("Ch", [128, 8192], BF16, offset=147712)
    Cl = nc.alloc_sbuf_tensor_at("Cl", [128, 8192], BF16, offset=164096)
    Cf = nc.alloc_sbuf_tensor_at("Cf", [128, 8192], F32, offset=147712)
    Dh = nc.alloc_sbuf_tensor_at("Dh", [128, 8192], BF16, offset=180480)
    Dl = nc.alloc_sbuf_tensor_at("Dl", [128, 8192], BF16, offset=196864)
    Df = nc.alloc_sbuf_tensor_at("Df", [128, 8192], F32, offset=180480)
    E = nc.alloc_sbuf_tensor_at("Ebuf", [128, E_COLS], F32, offset=213248)
    Eb = nc.alloc_sbuf_tensor_at("Ebf", [128, 1024], BF16, offset=226560)

    psum = [nc.alloc_psum_tensor(f"pb{i}", [128, 512], F32) for i in range(8)]

    bld = Builder()
    pool = Pool(psum)

    def col(t, o, n):
        return t[:, o:o + n]

    mz_b2 = bld.op("vector", lambda h: h.memset(col(Ps, 14400, 1984), 0.0))
    mz_e = bld.op("vector", lambda h: h.memset(col(E, 3181, E_COLS - 3181), 0.0))
    mz_st = mz_e
    mz_kv0 = bld.op("vector", lambda h: h.memset(col(E, E_KV(0), 1040), 0.0))
    mz_kv1 = bld.op("vector", lambda h: h.memset(col(E, E_KV(1), 1040), 0.0))

    om2_d = bld.dma("sync", "WSETUP", col(E, E_OM2, MF), om2)
    s16_d = bld.dma("sync", "WSETUP", col(E, E_S16(0), 128), s16d)
    id_d = bld.dma("sync", "WSETUP", col(E, E_ID, 128), idd)
    one_d = bld.dma("sync", "WSETUP", col(E, E_ONE, 1), oned)
    vec_d = bld.dma("sync", "WSETUP", col(E, E_VEC, 12), vec)
    cst_d = bld.dma("sync", "WSETUP", col(E, E_CST, 2), cstd, deps=[mz_e])
    wc_d = bld.dma("sync", "WSETUP",
                   col(E, E_WC, 512).rearrange("p (c f) -> p c f", c=8),
                   wcd.rearrange("(c p) f -> p c f", p=128))
    ID = col(E, E_ID, 128)

    def w8(src):
        return src.rearrange("(c p) n -> p c n", p=128)

    def v8(t, off):
        return t[:, off:off + 8192].rearrange("p (c n) -> p c n", c=8)

    def emit_split(dst_h, dst_l, src_ap, deps, lo_eng="vector",
                   hi_eng="scalar"):
        if hi_eng == "scalar":
            hi = bld.op("scalar", lambda h, d=dst_h, s=src_ap: h.copy(d, s),
                        deps=list(deps))
        else:
            hi = bld.op(hi_eng,
                        lambda h, d=dst_h, s=src_ap: h.tensor_copy(d, s),
                        deps=list(deps))
        lo = bld.op(lo_eng,
                    lambda h, d=dst_l, s=src_ap, dh=dst_h:
                    h.tensor_sub(d, s, dh),
                    deps=list(deps) + [hi])
        return hi, lo

    wk_st = bld.dma("sync", "WST", v8(Ps, P_STA), w8(wk))
    wv_st = bld.dma("sync", "WSU", v8(Ps, P_STB), w8(wv))
    _, wk_sp = emit_split(col(Ch, 0, 8192), col(Cl, 0, 8192),
                          col(Ps, P_STA, 8192), [wk_st])
    _, wv_sp = emit_split(col(Dh, 0, 8192), col(Dl, 0, 8192),
                          col(Ps, P_STB, 8192), [wv_st, mz_b2])
    _, wc_sp = emit_split(col(Eb, 0, 512), col(Eb, 512, 512),
                          col(E, E_WC, 512), [wc_d])

    dmyw = bld.dma("gpsimd", "BNW", dmy_i.ap(), xd[0:128, 0:8], deps=[])
    dmycc = bld.cc(lambda h: h.collective_compute(
        "AllReduce", ALU.add, replica_groups=[list(range(NCORES))],
        ins=[dmy_i.ap().opt()], outs=[dmy_o.ap().opt()]), deps=[dmyw])

    ln_state = {"prev_sqrt": None, "prev_var": None, "prev_nrm": None}
    ln2_state = {"mean2": None, "rv": None, "rsd": None, "rrc": None}

    def emit_ln(xt_ap, out_t, out_off, deps_x, junk_war=None, inplace=False):
        sm = col(E, E_LC, 1)
        sq = col(E, E_LC + 1, 1)
        mn = col(E, E_LC + 2, 1)
        mq = col(E, E_LC + 3, 1)
        vr = col(E, E_LC + 4, 1)
        sd = col(E, E_LC + 5, 1)
        rs = col(E, E_LC + 6, 1)
        out_ap = col(out_t, out_off, 1024)
        nrm_ap = xt_ap if inplace else out_ap
        r1 = bld.op("vector", lambda h: h.reduce_sum(out=sm, in_=xt_ap, axis=AX.X),
                    deps=list(deps_x) + [ln_state["prev_sqrt"], ln_state["prev_nrm"]])
        r2 = bld.op("scalar",
                    lambda h: h.activation(out=out_ap, in_=xt_ap, func=AF.Square,
                                           accum_out=sq),
                    deps=list(deps_x) + ([junk_war] if junk_war else [])
                    + [ln_state["prev_var"]])
        mean = bld.op("vector", lambda h: h.tensor_scalar_mul(mn, sm, 1.0 / 1024.0),
                      deps=[r1])
        msq = bld.op("vector", lambda h: h.tensor_mul(mq, mn, mn), deps=[mean])
        rv = bld.op("vector",
                    lambda h: h.tensor_scalar(out=vr, in0=sq, scalar1=1.0 / 1024.0,
                                              scalar2=None, op0=ALU.mult),
                    deps=[r2])
        rv2 = bld.op("vector", lambda h: h.tensor_sub(vr, vr, mq), deps=[rv, msq])
        rsd = bld.op("scalar",
                     lambda h: h.activation(out=sd, in_=vr, func=AF.Sqrt,
                                            bias=col(E, E_CST, 1)),
                     deps=[rv2, cst_d])
        rrc = bld.op("vector", lambda h: h.reciprocal(rs, sd), deps=[rsd])
        nrm = bld.op("vector",
                     lambda h: h.scalar_tensor_tensor(
                         out=nrm_ap, in0=xt_ap, scalar=mn,
                         in1=rs.to_broadcast([128, 1024]),
                         op0=ALU.subtract, op1=ALU.mult),
                     deps=[r2, mean, rrc])
        ln_state["prev_sqrt"] = rsd
        ln_state["prev_var"] = rv2
        ln_state["prev_nrm"] = nrm
        return r1, r2, nrm

    def emit_transpose_pair(src_ap_fn, dsth, dstl, dst_col, dep_src, evac_deps=()):
        los = []
        last = None
        for b in range(2):
            s = pool.acquire()
            for j in range(4):
                c = 4 * b + j
                last = bld.op(
                    "tensor",
                    lambda h, s=s, j=j, c=c: h.matmul(
                        s["t"][:, 128 * j:128 * (j + 1)], src_ap_fn(c), ID,
                        is_transpose=True, start=(j == 0), stop=(j == 3)),
                    deps=([dep_src, s["free"], id_d] if j == 0 else []))
            o3h = dsth[:, 0:16384].rearrange("p (c n) -> p c n", c=8)[
                :, 4 * b:4 * b + 4, dst_col:dst_col + 128]
            o3l = dstl[:, 0:16384].rearrange("p (c n) -> p c n", c=8)[
                :, 4 * b:4 * b + 4, dst_col:dst_col + 128]
            i3 = s["t"][:, 0:512].rearrange("p (c n) -> p c n", c=4)
            hi = bld.op("scalar", lambda h, o=o3h, i=i3: h.copy(o, i),
                        deps=[last] + list(evac_deps))
            lo = bld.op("vector",
                        lambda h, o=o3l, i=i3, dh=o3h: h.scalar_tensor_tensor(
                            out=o, in0=i, scalar=0.0, in1=dh,
                            op0=ALU.bypass, op1=ALU.subtract),
                        deps=[last, hi] + list(evac_deps))
            pool.release(s, lo)
            los.append(lo)
        return los, last

    def mm3(pt_ap, lh, ll, rh, rl, first, last_flag, deps, tile_position=None,
            deps_ll=()):
        kw = {} if tile_position is None else {"tile_position": tile_position}
        bld.op("tensor",
               lambda h, pt=pt_ap, a=lh, b=rh, kw=kw: h.matmul(
                   pt, a, b, start=first, stop=False, **kw),
               deps=list(deps))
        bld.op("tensor",
               lambda h, pt=pt_ap, a=lh, b=rl, kw=kw: h.matmul(
                   pt, a, b, start=False, stop=False, **kw), deps=[])
        r3 = bld.op("tensor",
                    lambda h, pt=pt_ap, a=ll, b=rh, kw=kw: h.matmul(
                        pt, a, b, start=False, stop=last_flag, **kw),
                    deps=list(deps_ll))
        return r3

    prev_x_readers = [wv_sp]
    prev_lns_war = wv_sp
    prev_kv_lastmm = None
    prev_kt_reader = wk_sp
    kv_last_adds = []
    kv_add_prev = {}
    pk_readers = [wv_sp, wv_sp, wv_sp, wv_sp]
    first_vevac_dep = wv_sp

    def emit_p1_xdma(s4):
        i3 = xd[512 * s4:512 * (s4 + 1), :].rearrange("(t p) n -> p t n", p=128)
        o3 = col(Ps, P_XS(0), 4096).rearrange("p (t n) -> p t n", t=4)
        p1_lnx["xdma"] = bld.dma("sync", "XP1", o3, i3,
                                 deps=p1_lnx["x_readers"] + [wv_sp])

    def emit_p1_ln(s4):
        tasks = []

        def _run_tasks_now():
            for f in tasks:
                f()
            del tasks[:]
        p1_lnx["flush"] = _run_tasks_now
        p1_lnx["tasks"] = tasks
        c = {}
        xdma = p1_lnx["xdma"]
        sm4 = col(E, E_LC4, 4).rearrange("p (t n) -> p t n", n=1)
        sq4 = col(E, E_LC4 + 4, 4)
        mn4 = col(E, E_LC4 + 8, 4)
        mq4 = col(E, E_LC4 + 12, 4)
        vr4 = col(E, E_LC4 + 16, 4)
        sd4 = col(E, E_LC4 + 20, 4)
        rs4 = col(E, E_LC4 + 24, 4)
        xall = col(Ps, P_XS(0), 4096).rearrange("p (t n) -> p t n", t=4)

        def t_r1():
            c["r1"] = bld.op(
                "vector",
                lambda h: h.reduce_sum(out=sm4, in_=xall, axis=AX.X),
                deps=[xdma, p1_lnx["mean4"]])
        tasks.append(t_r1)

        def mk_sq(t4):
            def f():
                xt = col(Ps, P_XS(t4), 1024)
                mul = bld.op("vector",
                             lambda h, xt=xt: h.tensor_mul(
                                 col(Ps, P_LNS, 1024), xt, xt),
                             deps=[xdma]
                             + ([p1_lnx["rv4"]] if t4 == 0
                                else [c[("sq", t4 - 1)]]))
                c[("sq", t4)] = bld.op(
                    "vector",
                    lambda h, t4=t4: h.reduce_sum(
                        out=E[:, E_LC4 + 4 + t4:E_LC4 + 5 + t4],
                        in_=col(Ps, P_LNS, 1024), axis=AX.X),
                    deps=[mul])
            return f
        for t4 in range(4):
            tasks.append(mk_sq(t4))

        def t_mid():
            mean4 = bld.op("vector",
                           lambda h: h.tensor_scalar_mul(
                               mn4, col(E, E_LC4, 4), 1.0 / 1024.0),
                           deps=[c["r1"]])
            msq4 = bld.op("vector",
                          lambda h: h.tensor_mul(mq4, mn4, mn4),
                          deps=[mean4])
            rv4 = bld.op("vector",
                         lambda h: h.tensor_scalar(
                             out=vr4, in0=sq4, scalar1=1.0 / 1024.0,
                             scalar2=None, op0=ALU.mult),
                         deps=[c[("sq", t4_)] for t4_ in range(4)]
                         + [p1_lnx["rsd4"]])
            rv4b = bld.op("vector",
                          lambda h: h.tensor_sub(vr4, vr4, mq4),
                          deps=[rv4, msq4])
            rsd4 = bld.op("scalar",
                          lambda h: h.activation(
                              out=sd4, in_=vr4, func=AF.Sqrt,
                              bias=col(E, E_CST, 1)),
                          deps=[rv4b, cst_d, p1_lnx["rrc4"]])
            c["rrc4"] = bld.op("vector",
                               lambda h: h.reciprocal(rs4, sd4),
                               deps=[rsd4])
            c["mean4"] = mean4
            c["rv4"] = rv4
            c["rsd4"] = rsd4
        tasks.append(t_mid)

        nrms = [None] * 4

        def mk_nrm(t4):
            def f():
                xt = col(Ps, P_XS(t4), 1024)
                nrms[t4] = bld.op(
                    "vector",
                    lambda h, xt=xt, t4=t4: h.scalar_tensor_tensor(
                        out=xt, in0=xt,
                        scalar=E[:, E_LC4 + 8 + t4:E_LC4 + 9 + t4],
                        in1=E[:, E_LC4 + 24 + t4:E_LC4 + 25 + t4]
                        .to_broadcast([128, 1024]),
                        op0=ALU.subtract, op1=ALU.mult),
                    deps=[c[("sq", t4)], c["mean4"], c["rrc4"]])
                if t4 == 3:
                    p1_lnx["nrms"] = nrms
                    p1_lnx["mean4"] = c["mean4"]
                    p1_lnx["rv4"] = c["rv4"]
                    p1_lnx["rsd4"] = c["rsd4"]
                    p1_lnx["rrc4"] = c["rrc4"]
            return f
        for t4 in range(4):
            tasks.append(mk_nrm(t4))

    p1_lnx = {"x_readers": [wv_sp], "nrms": None, "mean4": None,
               "rv4": None, "rsd4": None, "rrc4": None}
    wst_wq = {}
    wq_tasks = []
    emit_p1_xdma(0)
    emit_p1_ln(0)
    p1_lnx["flush"]()

    for s4 in range(NSL):
        nrms = p1_lnx["nrms"]
        slice_los = []
        tr_last = None
        for t4 in range(4):
            los, tr_last = emit_transpose_pair(
                lambda c, t4=t4: Ps[:, P_XS(t4) + 128 * c:
                                    P_XS(t4) + 128 * (c + 1)],
                Ah, Al, 512 * s4 + 128 * t4, nrms[t4])
            slice_los += los
        p1_lnx["x_readers"] = [tr_last]

        kt_evacs = []
        for mc in range(8):
            s = pool.acquire()
            last = None
            for kc in range(8):
                deps = []
                if kc == 0:
                    deps = [s["free"], wk_sp] + slice_los
                last = mm3(
                    s["t"][:, 0:512],
                    Ch[:, 1024 * kc + 128 * mc:1024 * kc + 128 * mc + 128],
                    Cl[:, 1024 * kc + 128 * mc:1024 * kc + 128 * mc + 128],
                    Ah[:, A_LN1T(kc) + 512 * s4:A_LN1T(kc) + 512 * s4 + 512],
                    Al[:, A_LN1T(kc) + 512 * s4:A_LN1T(kc) + 512 * s4 + 512],
                    first=(kc == 0), last_flag=(kc == 7), deps=deps)
            ev = bld.op("scalar",
                        lambda h, s=s, mc=mc: h.copy(
                            Ps[:, P_KT(mc):P_KT(mc) + 512], s["t"][:, 0:512]),
                        deps=[last, prev_kt_reader, wk_sp])
            pool.release(s, ev)
            kt_evacs.append(ev)
        k2 = bld.op("scalar",
                    lambda h: h.activation(out=col(Ps, 4096, 4096),
                                           in_=col(Ps, 0, 4096), func=AF.Square),
                    deps=[prev_kv_lastmm, kt_evacs[-1], wv_sp])
        knorm_last = None
        nk_evs = []
        for t4 in range(4):
            s = pool.acquire()
            last = None
            for c in range(8):
                deps = [k2, s16_d, s["free"]] if c == 0 else []
                last = bld.op(
                    "tensor",
                    lambda h, s=s, c=c, t4=t4: h.matmul(
                        s["t"][:, 0:16],
                        Ps[:, 4096 + 512 * c + 128 * t4:
                           4096 + 512 * c + 128 * t4 + 128],
                        col(E, E_S16(c), 16), start=(c == 0), stop=(c == 7)),
                    deps=deps)
            ev = bld.op("scalar",
                        lambda h, s=s, t4=t4: h.activation(
                            out=E[:, E_NK + 16 * t4:E_NK + 16 * t4 + 16],
                            in_=s["t"][:, 0:16], func=AF.Identity,
                            scale=-0.5, bias=col(E, E_CST + 1, 1)),
                        deps=[last, cst_d])
            pool.release(s, ev)
            knorm_last = last
            nk_evs.append(ev)
        v_evacs = {}
        for t4 in range(4):
            for half in range(2):
                s = pool.acquire()
                last = None
                for kc in range(8):
                    deps = [s["free"], wv_sp] if kc == 0 else []
                    if kc == 0 and t4 == 0 and half == 0:
                        deps += slice_los
                    last = mm3(
                        s["t"][:, 0:512],
                        Ah[:, A_LN1T(kc) + 512 * s4 + 128 * t4:
                           A_LN1T(kc) + 512 * s4 + 128 * t4 + 128],
                        Al[:, A_LN1T(kc) + 512 * s4 + 128 * t4:
                           A_LN1T(kc) + 512 * s4 + 128 * t4 + 128],
                        Dh[:, 1024 * kc + 512 * half:1024 * kc + 512 * half + 512],
                        Dl[:, 1024 * kc + 512 * half:1024 * kc + 512 * half + 512],
                        first=(kc == 0), last_flag=(kc == 7), deps=deps)
                o3v = col(Ps, P_V(t4) + 520 * half, 520).rearrange(
                    "p (hh d) -> p hh d", hh=8)[:, :, 0:64]
                i3v = s["t"][:, 0:512].rearrange("p (hh d) -> p hh d", hh=8)
                ev = bld.op("scalar",
                            lambda h, s=s, o3v=o3v, i3v=i3v: h.copy(o3v, i3v),
                            deps=[last, knorm_last, first_vevac_dep])
                pool.release(s, ev)
                v_evacs[(t4, half)] = ev
        first_vevac_dep = None
        vones = []
        for t4 in range(4):
            ap1 = col(Ps, P_V(t4), 1040).rearrange(
                "p (hh d) -> p hh d", hh=16)[:, :, 64:65]
            vones.append(bld.op("vector",
                                lambda h, ap1=ap1: h.memset(ap1, 1.0),
                                deps=[k2, v_evacs[(t4, 0)], v_evacs[(t4, 1)],
                                      prev_kv_lastmm]))
        if s4 + 1 < NSL:
            emit_p1_xdma(s4 + 1)
            emit_p1_ln(s4 + 1)
        if s4 == NSL - 1:
            wst_wq["dma"] = bld.dma("sync", "WQ", v8(Df, 0), w8(wq),
                                    deps=[last])

            def mk_wq_split(k):
                def f():
                    _, lo = emit_split(
                        col(Ch, 1024 * k, 1024), col(Cl, 1024 * k, 1024),
                        col(Df, 1024 * k, 1024),
                        [wst_wq["dma"], kt_evacs[-1]])
                    wst_wq["sp"] = lo
                return f
            for k in range(8):
                wq_tasks.append(mk_wq_split(k))
        for hh in range(NH):
            if hh >= 4 and p1_lnx.get("tasks"):
                for f in p1_lnx["tasks"][:2]:
                    f()
                del p1_lnx["tasks"][:2]
            if hh >= 8 and wq_tasks:
                wq_tasks.pop(0)()
            b0 = 64 * (hh % 2)
            kvs = [pool.acquire(), pool.acquire()]
            lastmm = [None, None]

            def emit_pj(t4, hh=hh, b0=b0):
                ps = pool.acquire()
                pidx = (hh * 4 + t4) % 4
                pj = bld.op(
                    "tensor",
                    lambda h, ps=ps, hh=hh, t4=t4, b0=b0: h.matmul(
                        ps["t"][:, 0:256],
                        Ps[b0:b0 + 64, P_KT(hh // 2) + 128 * t4:
                           P_KT(hh // 2) + 128 * t4 + 128],
                        E[b0:b0 + 64, E_OM2:E_OM2 + 256],
                        start=True, stop=True),
                    deps=[ps["free"], om2_d, kt_evacs[hh // 2]])
                pk = bld.op(
                    "scalar",
                    lambda h, ps=ps, pidx=pidx, t4=t4, hh=hh: h.activation(
                        out=col(Ps, P_PK(pidx), 256), in_=ps["t"][:, 0:256],
                        func=AF.Exp, bias=E[:, E_NK + 16 * t4 + hh:
                                            E_NK + 16 * t4 + hh + 1], scale=1.0),
                    deps=[pj, pk_readers[pidx], nk_evs[t4]])
                pool.release(ps, pk)
                return pk

            pks = [emit_pj(0), emit_pj(1)]
            for t4 in range(4):
                pidx = (hh * 4 + t4) % 4
                pk = pks[t4]
                if t4 + 2 < 4:
                    pks.append(emit_pj(t4 + 2))
                for mc in range(2):
                    kb = kvs[mc]
                    deps = [pk, v_evacs[(t4, (64 * hh) // 512)], vones[t4]]
                    if t4 == 0:
                        deps.append(kb["free"])
                    zm = bld.op("tensor",
                                lambda h, kb=kb, pidx=pidx, mc=mc, t4=t4, hh=hh: h.matmul(
                                    kb["t"][:, 0:65],
                                    Ps[:, P_PK(pidx) + 128 * mc:
                                       P_PK(pidx) + 128 * mc + 128],
                                    Ps[:, P_V(t4) + 65 * hh:P_V(t4) + 65 * hh + 65],
                                    start=(t4 == 0), stop=(t4 == 3)),
                                deps=deps)
                    lastmm[mc] = zm
                pk_readers[pidx] = lastmm[1]
            for mc in range(2):
                dst = E[:, E_KV(mc) + 65 * hh:E_KV(mc) + 65 * hh + 65]
                add = bld.op("vector",
                             lambda h, dst=dst, kb=kvs[mc]: h.tensor_add(
                                 dst, dst, kb["t"][:, 0:65]),
                             deps=[lastmm[mc], mz_kv0, mz_kv1,
                                   kv_add_prev.get((mc, hh))])
                kv_add_prev[(mc, hh)] = add
                pool.release(kvs[mc], add)
                kv_last_adds.append(add)
        if s4 + 1 < NSL:
            p1_lnx["flush"]()
        prev_kv_lastmm = pk_readers[(NH * 4 - 1) % 4]
        prev_kt_reader = prev_kv_lastmm

    if stop == "p1":
        _dump_and_finish(nc, bld, [("P", Ps, 0, 16384), ("E", E, 0, E_COLS)])
        return nc

    kv_done = kv_last_adds[-1]
    p1_lastT = prev_kv_lastmm
    wq_sp = wst_wq["sp"]

    x2_los = {}
    x1_writes = {}
    xt_war = [[], []]
    sq_war = [mz_e, mz_e]
    pq_war = [None, None]
    q2_war = None
    qt_reader_prev = None
    stats_last = None
    p2_lastS = None
    stat_prev = {}
    tmp1_war = None
    tmp2_war = None
    p2_first = [p1_lastT, kv_done, wq_sp]
    qt_a_last = None
    qt_lastmm = None
    wst = {}
    p3w_tasks = {}

    for s4 in range(NSL):
        qt_evacs = []
        for c in range(8):
            s = pool.acquire()
            last = None
            for kc in range(8):
                deps = [s["free"], wq_sp] if kc == 0 else []
                last = mm3(
                    s["t"][:, 0:512],
                    Ch[:, 1024 * kc + 128 * c:1024 * kc + 128 * c + 128],
                    Cl[:, 1024 * kc + 128 * c:1024 * kc + 128 * c + 128],
                    Ah[:, A_LN1T(kc) + 512 * s4:A_LN1T(kc) + 512 * s4 + 512],
                    Al[:, A_LN1T(kc) + 512 * s4:A_LN1T(kc) + 512 * s4 + 512],
                    first=(kc == 0), last_flag=(kc == 7), deps=deps)
            ev = bld.op("scalar",
                        lambda h, s=s, c=c: h.copy(
                            Df[:, D_QT(c):D_QT(c) + 512], s["t"][:, 0:512]),
                        deps=[last, qt_reader_prev]
                        + (p2_first if (s4 == 0 and c == 0) else []))
            pool.release(s, ev)
            qt_evacs.append(ev)
        if s4 == NSL - 1:
            qt_a_last = last
            wparts = [bld.dma(
                "scalar", "WGT",
                col(Cf, 2048 * k, 2048).rearrange("p (c n) -> p c n", c=2),
                w8(wgt)[:, 2 * k:2 * k + 2], deps=[qt_a_last])
                for k in range(4)]
            wch = {}

            def _wgt_chunks(k0, k1):
                def f():
                    for k in range(k0, k1):
                        _, lo = emit_split(
                            col(Ah, A_W(k), 1024), col(Al, A_W(k), 1024),
                            col(Cf, 1024 * k, 1024),
                            [wparts[k // 2], qt_a_last],
                            lo_eng="gpsimd", hi_eng="gpsimd")
                        wch[k] = lo
                        wst["wgt_sp"] = lo
                return f

            def _slot14():
                _wgt_chunks(2, 5)()
                wst["lw_d"] = {}
                for br, src in ((0, lwso), (1, lwsi)):
                    i3b = src.rearrange("(T p) n -> p T n", p=128)
                    stg = col(Cf, 2048 * br, 2048).rearrange(
                        "p (T n) -> p T n", T=2)
                    wst["lw_d"][br] = bld.dma("scalar", "WLW", stg, i3b,
                                              deps=[qt_a_last, wch[3]])

            def _slot15():
                _wgt_chunks(5, 8)()
                wst[("lb_d", 0)] = bld.dma("scalar", "WLB",
                                           col(Cf, C_LB(0), 1024), lbso,
                                           deps=[qt_a_last, wch[7]])
                wst[("lb_d", 1)] = bld.dma("scalar", "WLB",
                                           col(Cf, C_LB(1), 1024), lbsi,
                                           deps=[qt_a_last, wch[7]])
                for br in range(2):
                    sp = None
                    for k in range(2):
                        _, sp = emit_split(
                            col(Ah, A_LW(br, k), 1024),
                            col(Al, A_LW(br, k), 1024),
                            col(Cf, 2048 * br + 1024 * k, 1024),
                            [wst["lw_d"][br], wch[3]],
                            lo_eng="gpsimd", hi_eng="gpsimd")
                    wst[("lw_sp", br)] = sp

            wst["wgt_st"] = wparts[3]
            p3w_tasks = {13: _wgt_chunks(0, 2), 14: _slot14, 15: _slot15}
        qn_state = {"last": None}

        def emit_qn(t, first_of_pair):
            t4 = t - 4 * s4
            qsrc = Df[:, 0:4096].rearrange("p (c n) -> p c n", c=8)[
                :, :, 128 * t4:128 * t4 + 128]
            q2deps = [q2_war, qt_evacs[-1]]
            if first_of_pair:
                q2deps += [pq_war[0], pq_war[1]]
            else:
                q2deps.append(qn_state["last"])
            q2 = bld.op("scalar",
                        lambda h, qsrc=qsrc: h.activation(
                            out=col(Df, D_Q2, 1024), in_=qsrc,
                            func=AF.Square),
                        deps=q2deps)
            s = pool.acquire()
            last = None
            for c in range(8):
                deps = [q2, s["free"]] if c == 0 else []
                last = bld.op(
                    "tensor",
                    lambda h, s=s, c=c: h.matmul(
                        s["t"][:, 0:16],
                        Df[:, D_Q2 + 128 * c:D_Q2 + 128 * (c + 1)],
                        col(E, E_S16(c), 16), start=(c == 0),
                        stop=(c == 7)),
                    deps=deps)
            qn_state["last"] = last
            ev = bld.op("scalar",
                        lambda h, s=s, t=t: h.activation(
                            out=col(E, E_SQ(t), 16),
                            in_=s["t"][:, 0:16],
                            func=AF.Exp, scale=-0.5),
                        deps=[last, sq_war[t % 2]])
            pool.release(s, ev)
            qn_state[t] = ev

        for pr in range(2):
            pair = (4 * s4 + 2 * pr, 4 * s4 + 2 * pr + 1)
            xdmas = {}
            sqevs = {}
            for t in pair:
                tk = p3w_tasks.pop(t, None)
                if tk is not None:
                    tk()
                xdmas[t] = bld.dma("sync", "XT%d" % (t % 2),
                                   col(Df, D_XT(t), 1024),
                                   xd[128 * t:128 * (t + 1), :],
                                   deps=list(xt_war[t % 2])
                                   + (p2_first if t < 2 else []))
                if t not in qn_state:
                    emit_qn(t, first_of_pair=(t == pair[0]))
                sqevs[t] = qn_state.pop(t)
            sq_last = {pair[0]: None, pair[1]: None}
            last_xu = {pair[0]: None, pair[1]: None}

            def emit_proj(hh, pr=pr):
                b0 = 64 * (hh % 2)
                ps = pool.acquire()
                mm2 = None
                for c in range(2):
                    mm2 = bld.op(
                        "tensor",
                        lambda h, ps=ps, hh=hh, pr=pr, b0=b0, c=c: h.matmul(
                            ps["t"][:, 256 * c:256 * c + 256],
                            E[b0:b0 + 64, E_OM2 + 128 * c:
                              E_OM2 + 128 * c + 128],
                            Df[b0:b0 + 64, D_QT(hh // 2) + 256 * pr:
                               D_QT(hh // 2) + 256 * pr + 256],
                            start=(c == 0), stop=(c == 1)),
                        deps=([ps["free"], qt_evacs[hh // 2]]
                              if c == 0 else []))
                pq = bld.op("scalar",
                            lambda h, ps=ps, hh=hh: h.activation(
                                out=col(Df, D_PQS(hh), 512),
                                in_=ps["t"][:, 0:512],
                                func=AF.Exp, bias=col(E, E_CST + 1, 1),
                                scale=1.0),
                            deps=[mm2, pq_war[hh % 2], cst_d])
                pool.release(ps, pq)
                return mm2, pq

            nxt_pq = emit_proj(0)
            for hh in range(NH):
                qt_lastmm, pq = nxt_pq
                if hh + 1 < NH:
                    nxt_pq = emit_proj(hh + 1)
                o4 = None
                for t in pair:
                    ti = t % 2
                    kz = pool.acquire()
                    bld.op("tensor",
                           lambda h, kz=kz, hh=hh, ti=ti: h.matmul(
                               kz["t"][:, 0:65],
                               Df[:, D_PQS(hh) + 128 * ti:
                                  D_PQS(hh) + 128 * ti + 128],
                               E[:, E_KV(0) + 65 * hh:E_KV(0) + 65 * hh + 65],
                               start=True, stop=False),
                           deps=[pq, kv_done, kz["free"]])
                    o4 = bld.op("tensor",
                                lambda h, kz=kz, hh=hh, ti=ti: h.matmul(
                                    kz["t"][:, 0:65],
                                    Df[:, D_PQS(hh) + 256 + 128 * ti:
                                       D_PQS(hh) + 256 + 128 * ti + 128],
                                    E[:, E_KV(1) + 65 * hh:
                                      E_KV(1) + 65 * hh + 65],
                                    start=False, stop=True), deps=[])
                    sqc = E[:, E_SQ(t) + hh:E_SQ(t) + hh + 1]
                    zc = col(E, E_ZC, 1)
                    rc = col(E, E_ZC + 1, 1)
                    fc = col(E, E_ZC + 2, 1)
                    zcr = bld.op("vector",
                                 lambda h, kz=kz, sqc=sqc, zc=zc: h.tensor_scalar(
                                     out=zc, in0=kz["t"][:, 64:65], scalar1=sqc,
                                     scalar2=ZEPS, op0=ALU.mult, op1=ALU.add),
                                 deps=[o4, sqevs[t]])
                    rop = bld.op("vector",
                                 lambda h, zc=zc, rc=rc: h.reciprocal(rc, zc),
                                 deps=[zcr])
                    fop = bld.op("vector",
                                 lambda h, rc=rc, fc=fc, sqc=sqc: h.tensor_scalar(
                                     out=fc, in0=rc, scalar1=sqc, scalar2=None,
                                     op0=ALU.mult), deps=[rop, sqevs[t]])
                    xu = bld.op("vector",
                                lambda h, kz=kz, fc=fc, t=t, hh=hh:
                                h.scalar_tensor_tensor(
                                    out=Df[:, D_XT(t) + 64 * hh:
                                           D_XT(t) + 64 * hh + 64],
                                    in0=kz["t"][:, 0:64], scalar=fc,
                                    in1=Df[:, D_XT(t) + 64 * hh:
                                           D_XT(t) + 64 * hh + 64],
                                    op0=ALU.mult, op1=ALU.add),
                                deps=[xdmas[t], fop])
                    pool.release(kz, xu)
                    sq_last[t] = fop
                    last_xu[t] = xu
                pq_war[hh % 2] = o4
                p2_o4_last = o4
            t0, t1 = pair
            sq_war[t0 % 2] = sq_last[t0]
            sq_war[t1 % 2] = sq_last[t1]
            t0ap = col(Df, D_XT(t0), 1024)
            t1ap = col(Df, D_XT(t1), 1024)
            sm2 = col(E, E_LC2, 2).rearrange("p (t n) -> p t n", n=1)
            sqa = col(E, E_LC2 + 2, 1)
            sqb = col(E, E_LC2 + 3, 1)
            sq2 = col(E, E_LC2 + 2, 2)
            mn2 = col(E, E_LC2 + 4, 2)
            mq2 = col(E, E_LC2 + 6, 2)
            vr2 = col(E, E_LC2 + 8, 2)
            sd2 = col(E, E_LC2 + 10, 2)
            rs2c = col(E, E_LC2 + 12, 2)
            xpair = Df[:, D_XT(t0):D_XT(t0) + 2048].rearrange(
                "p (t n) -> p t n", t=2)
            r1 = bld.op("vector",
                        lambda h: h.reduce_sum(out=sm2, in_=xpair, axis=AX.X),
                        deps=[last_xu[t0], last_xu[t1], ln2_state["mean2"]])
            r2a = bld.op("scalar",
                         lambda h: h.activation(
                             out=col(Df, D_Q2, 1024), in_=t0ap,
                             func=AF.Square, accum_out=sqa),
                         deps=[last_xu[t0], p2_o4_last, ln2_state["rv"]])
            r2b = bld.op("scalar",
                         lambda h: h.activation(
                             out=col(Df, D_Q2, 1024), in_=t1ap,
                             func=AF.Square, accum_out=sqb),
                         deps=[last_xu[t1], r2a, ln2_state["rv"]])
            mean2 = bld.op("vector",
                           lambda h: h.tensor_scalar_mul(
                               mn2, col(E, E_LC2, 2), 1.0 / 1024.0),
                           deps=[r1])
            msq2 = bld.op("vector",
                          lambda h: h.tensor_mul(mq2, mn2, mn2), deps=[mean2])
            rv = bld.op("vector",
                        lambda h: h.tensor_scalar(
                            out=vr2, in0=sq2, scalar1=1.0 / 1024.0,
                            scalar2=None, op0=ALU.mult),
                        deps=[r2a, r2b, ln2_state["rsd"]])
            rv2 = bld.op("vector",
                         lambda h: h.tensor_sub(vr2, vr2, mq2),
                         deps=[rv, msq2])
            rsd = bld.op("scalar",
                         lambda h: h.activation(
                             out=sd2, in_=vr2, func=AF.Sqrt,
                             bias=col(E, E_CST, 1)),
                         deps=[rv2, cst_d, ln2_state["rrc"]])
            rrc = bld.op("vector",
                         lambda h: h.reciprocal(rs2c, sd2), deps=[rsd])
            ln2_state["mean2"] = mean2
            ln2_state["rv"] = rv
            ln2_state["rsd"] = rsd
            ln2_state["rrc"] = rrc

            def emit_nrm(ti, tap, war):
                return bld.op(
                    "vector",
                    lambda h, ti=ti, tap=tap: h.scalar_tensor_tensor(
                        out=col(Df, D_Q2, 1024), in0=tap,
                        scalar=E[:, E_LC2 + 4 + ti:E_LC2 + 5 + ti],
                        in1=E[:, E_LC2 + 12 + ti:E_LC2 + 13 + ti]
                        .to_broadcast([128, 1024]),
                        op0=ALU.subtract, op1=ALU.mult),
                    deps=[r2a, r2b, mean2, rrc] + list(war))
            nrm0 = emit_nrm(0, t0ap, [])
            x1w0 = bld.dma("gpsimd", "XW%d" % (t0 % 2),
                           x1d[128 * t0:128 * (t0 + 1), :], t0ap,
                           deps=[nrm0, r2a])
            x1_writes[t0] = x1w0
            xt_war[t0 % 2] = [x1w0]
            x1w1 = bld.dma("gpsimd", "XW%d" % (t1 % 2),
                           x1d[128 * t1:128 * (t1 + 1), :], t1ap,
                           deps=[r2b, r1])
            x1_writes[t1] = x1w1
            xt_war[t1 % 2] = [x1w1]

            def emit_conv(t):
                nonlocal tmp1_war, tmp2_war, p2_lastS, stats_last
                for br in range(2):
                    s = pool.acquire()
                    last = None
                    for c in range(8):
                        deps = [s["free"], wc_sp, x2_los[t][c // 4]] \
                            if c == 0 else [x2_los[t][c // 4]]
                        last = mm3(
                            s["t"][32 * (c % 4):32 * (c % 4) + 32,
                                   128 * (c // 4):128 * (c // 4) + 128],
                            Eb[:, EB_WC(c, br, 0):EB_WC(c, br, 0) + 32],
                            Eb[:, EB_WC(c, br, 1):EB_WC(c, br, 1) + 32],
                            Bh[:, B_X2(c) + 128 * t:B_X2(c) + 128 * t + 128],
                            Bl[:, B_X2(c) + 128 * t:B_X2(c) + 128 * t + 128],
                            first=True, last_flag=True, deps=deps,
                            tile_position=(0, 32 * (c % 4)))
                    xcb_evs = []
                    for Th in range(2):
                        cbcol = E[:, E_VEC + 6 * br + Th:
                                  E_VEC + 6 * br + Th + 1]
                        evb = bld.op("scalar",
                                     lambda h, s=s, br=br, Th=Th, cbcol=cbcol:
                                     h.activation(
                                         out=Df[:, D_XCB(br) + 128 * Th:
                                                D_XCB(br) + 128 * Th + 128],
                                         in_=s["t"][:, 128 * Th:128 * Th + 128],
                                         func=AF.Identity, bias=cbcol,
                                         scale=1.0),
                                     deps=[last, vec_d])
                        xcb_evs.append(evb)
                    pool.release(s, xcb_evs[-1])
                    for Th in range(2):
                        tmp1 = col(E, E_LC + 8, 1)
                        tmp2 = col(E, E_LC + 9, 1)
                        stc1 = E[:, E_ST + 4 * br + Th:E_ST + 4 * br + Th + 1]
                        stc2 = E[:, E_ST + 4 * br + 2 + Th:
                                 E_ST + 4 * br + 2 + Th + 1]
                        xcb_ap = Df[:, D_XCB(br) + 128 * Th:
                                    D_XCB(br) + 128 * Th + 128]
                        rsm = bld.op("vector",
                                     lambda h, xcb_ap=xcb_ap, tmp1=tmp1:
                                     h.reduce_sum(
                                         out=tmp1, in_=xcb_ap, axis=AX.X),
                                     deps=[xcb_evs[Th],
                                           stat_prev.get(("s", br, Th)),
                                           tmp1_war])
                        a1 = bld.op("vector",
                                    lambda h, stc1=stc1, tmp1=tmp1:
                                    h.tensor_add(stc1, stc1, tmp1),
                                    deps=[mz_st, rsm,
                                          stat_prev.get(("s", br, Th))])
                        stat_prev[("s", br, Th)] = a1
                        tmp1_war = a1
                        sqx = bld.op("scalar",
                                     lambda h, xcb_ap=xcb_ap, tmp2=tmp2:
                                     h.activation(
                                         out=xcb_ap, in_=xcb_ap,
                                         func=AF.Square, accum_out=tmp2),
                                     deps=[rsm, xcb_evs[Th],
                                           stat_prev.get(("q", br, Th)),
                                           tmp2_war])
                        p2_lastS = sqx
                        stats_last = bld.op(
                            "vector",
                            lambda h, stc2=stc2, tmp2=tmp2: h.tensor_add(
                                stc2, stc2, tmp2),
                            deps=[sqx, mz_st, stat_prev.get(("q", br, Th))])
                        stat_prev[("q", br, Th)] = stats_last
                        tmp2_war = stats_last

            los0, trl0 = emit_transpose_pair(
                lambda c: Df[:, D_Q2 + 128 * c:D_Q2 + 128 * (c + 1)],
                Bh, Bl, 128 * t0, nrm0,
                evac_deps=(p2_first if t0 == 0 else []))
            x2_los[t0] = los0
            nrm1 = emit_nrm(1, t1ap, [trl0])
            xt_war[t1 % 2].append(nrm1)
            emit_conv(t0)
            los1, trl1 = emit_transpose_pair(
                lambda c: Df[:, D_Q2 + 128 * c:D_Q2 + 128 * (c + 1)],
                Bh, Bl, 128 * t1, nrm1)
            x2_los[t1] = los1
            q2_war = trl1
            if pr == 0:
                emit_qn(pair[1] + 1, first_of_pair=True)
            emit_conv(t1)
            if pr == 0:
                emit_qn(pair[1] + 2, first_of_pair=False)
    qt_reader_prev = qt_lastmm

    bnw = bld.dma("gpsimd", "BNW", bni.ap(), col(E, E_ST, 8), deps=[stats_last])
    ccop = bld.cc(lambda h: h.collective_compute(
        "AllReduce", ALU.add, replica_groups=[list(range(NCORES))],
        ins=[bni.ap().opt()], outs=[bno.ap().opt()]), deps=[bnw, dmycc])
    bnr = bld.dma("gpsimd", "BNR", col(E, E_BO, 8), bno.ap(), deps=[ccop])
    bm_refs = []
    for br in range(2):
        sums = col(E, E_BO + 4 * br, 2)
        sqs = col(E, E_BO + 4 * br + 2, 2)
        mn = col(E, E_BS, 2)
        mq = col(E, E_BS + 2, 2)
        vr = col(E, E_BS + 4, 2)
        sd = col(E, E_BS + 6, 2)
        gam = col(E, E_VEC + 6 * br + 2, 2)
        cbc = col(E, E_VEC + 6 * br, 2)
        meanb = col(E, E_BM + 4 * br, 2)
        sstar = col(E, E_BM + 4 * br + 2, 2)
        r0 = bld.op("vector", lambda h, mn=mn, sums=sums: h.tensor_scalar_mul(
            mn, sums, 1.0 / NALL), deps=[bnr, bm_refs[-1] if bm_refs else None])
        m1 = bld.op("vector", lambda h, mq=mq, mn=mn: h.tensor_mul(mq, mn, mn),
                    deps=[r0])
        m2 = bld.op("vector", lambda h, vr=vr, sqs=sqs: h.tensor_scalar_mul(
            vr, sqs, 1.0 / NALL), deps=[bnr])
        rv = bld.op("vector", lambda h, vr=vr, mq=mq: h.tensor_sub(vr, vr, mq),
                    deps=[m1, m2])
        rsd = bld.op("scalar", lambda h, sd=sd, vr=vr: h.activation(
            out=sd, in_=vr, func=AF.Sqrt, bias=col(E, E_CST, 1)), deps=[rv])
        rrc = bld.op("vector", lambda h, sd=sd: h.reciprocal(sd, sd), deps=[rsd])
        rs2 = bld.op("vector", lambda h, sstar=sstar, sd=sd, gam=gam: h.tensor_mul(
            sstar, sd, gam), deps=[vec_d, rrc])
        rmb = bld.op("vector", lambda h, meanb=meanb, mn=mn, cbc=cbc: h.tensor_sub(
            meanb, mn, cbc), deps=[r0, vec_d])
        tmpb = col(E, E_BS, 2)
        beta2 = col(E, E_VEC + 6 * br + 4, 2)
        nbc = col(E, E_NB + 2 * br, 2)
        rt = bld.op("vector", lambda h, tmpb=tmpb, meanb=meanb, sstar=sstar:
                    h.tensor_mul(tmpb, meanb, sstar), deps=[rmb, rs2])
        rnb = bld.op("vector", lambda h, nbc=nbc, beta2=beta2, tmpb=tmpb:
                     h.tensor_sub(nbc, beta2, tmpb), deps=[rt, vec_d, mz_e])
        bm_refs += [rrc, rs2, rmb, rt, rnb]
    bm_done = bm_refs[-1]

    if stop == "p2":
        _dump_and_finish(nc, bld, [("Bh", Bh, 0, 16384), ("Bl", Bl, 0, 16384),
                                   ("E", E, 0, E_COLS)])
        return nc

    wgt_st = wst["wgt_st"]
    wgt_sp = wst["wgt_sp"]
    lw_sp = {0: wst[("lw_sp", 0)], 1: wst[("lw_sp", 1)]}
    cf_ready = [wgt_sp, lw_sp[0], lw_sp[1]]
    lb_d = {0: wst[("lb_d", 0)], 1: wst[("lb_d", 1)]}
    bb_d = bld.dma("sync", "WLB", col(Df, D_BIAS, 1024), bbc,
                   deps=[qt_lastmm, p2_lastS])

    pst = {
        "xa_war": [None, None], "sc_war": [None, None], "pu_war": [None, None],
        "expp": [None, None], "gel_war": [None, None], "x1_readers": [None, None],
        "mi_war": None, "xgt_war": None, "tmp2_war": None, "obt_war": None,
        "w_war": [None, None],
        "fb_war": {(p, n): None for p in range(2) for n in range(2)},
    }
    out_dmas = []

    def stage_amm(t):
        a = {"t": t}
        if t < 2:
            a["x1r"] = bld.dma("gpsimd", "XR%d" % (t % 2),
                               col(Df, D_X1(t), 1024),
                               x1d[128 * t:128 * (t + 1), :],
                               deps=[x1_writes[t], qt_lastmm, p2_lastS])
        a["scev"] = {}
        for br in range(2):
            s = pool.acquire()
            last = None
            for c in range(8):
                deps = [s["free"], wc_sp, x2_los[t][c // 4]] if c == 0 else [
                    x2_los[t][c // 4]]
                last = mm3(
                    s["t"][32 * (c % 4):32 * (c % 4) + 32,
                           128 * (c // 4):128 * (c // 4) + 128],
                    Eb[:, EB_WC(c, br, 0):EB_WC(c, br, 0) + 32],
                    Eb[:, EB_WC(c, br, 1):EB_WC(c, br, 1) + 32],
                    Bh[:, B_X2(c) + 128 * t:B_X2(c) + 128 * t + 128],
                    Bl[:, B_X2(c) + 128 * t:B_X2(c) + 128 * t + 128],
                    first=True, last_flag=True, deps=deps,
                    tile_position=(0, 32 * (c % 4)))
            gl = []
            for Th in range(2):
                sstar = E[:, E_BM + 4 * br + 2 + Th:E_BM + 4 * br + 2 + Th + 1]
                nbc = E[:, E_NB + 2 * br + Th:E_NB + 2 * br + Th + 1]
                g = bld.op("scalar",
                           lambda h, s=s, br=br, Th=Th, sstar=sstar, nbc=nbc:
                           h.activation(
                               out=Df[:, D_XA32(br) + 128 * Th:
                                      D_XA32(br) + 128 * Th + 128],
                               in_=s["t"][:, 128 * Th:128 * Th + 128],
                               func=AF.Gelu, bias=nbc, scale=sstar),
                           deps=[last, bm_done, pst["gel_war"][br]]
                           + ([x1_writes[14], x1_writes[15], qt_lastmm, p2_lastS]
                              if (t == 0 and br == 0 and Th == 0) else []))
                gl.append(g)
            pool.release(s, gl[-1])
            xah, xal = emit_split(
                col(Ah, A_XA(br), 256), col(Al, A_XA(br), 256),
                col(Df, D_XA32(br), 256), [gl[0], gl[1], pst["xa_war"][br]],
                lo_eng="gpsimd")
            a["xal%d" % br] = xal
            a["xah%d" % br] = xah
            pst["gel_war"][br] = xal
        return a

    def stage_amm2a(a):
        t = a["t"]
        a["fb"] = []
        for nch in range(2):
            s = pool.acquire()
            last = None
            for kc in range(8):
                deps = [s["free"], wgt_sp, x2_los[t][kc // 4]] if kc == 0 else [
                    x2_los[t][kc // 4]]
                last = mm3(
                    s["t"][:, 0:512],
                    Bh[:, B_X2(kc) + 128 * t:B_X2(kc) + 128 * t + 128],
                    Bl[:, B_X2(kc) + 128 * t:B_X2(kc) + 128 * t + 128],
                    Ah[:, A_W(kc) + 512 * nch:A_W(kc) + 512 * nch + 512],
                    Al[:, A_W(kc) + 512 * nch:A_W(kc) + 512 * nch + 512],
                    first=(kc == 0), last_flag=(kc == 7), deps=deps)
            fbc = bld.op("scalar",
                         lambda h, s=s, t=t, nch=nch: h.copy(
                             col(E, E_FB(t, nch), 512), s["t"][:, 0:512]),
                         deps=[last, pst["fb_war"][(t % 2, nch)]]
                         + ([p2_o4_last] if t < 2 else []))
            pool.release(s, fbc)
            a["fb"].append(fbc)
        return a

    def stage_amm2b(a):
        t = a["t"]
        for br in range(2):
            xah, xal = a["xah%d" % br], a["xal%d" % br]
            for nch in range(2):
                s2 = pool.acquire()
                last2 = None
                for kc in range(2):
                    deps = [s2["free"], lw_sp[br], xah] if kc == 0 else []
                    last2 = mm3(
                        s2["t"][:, 0:512],
                        Ah[:, A_XA(br) + 128 * kc:A_XA(br) + 128 * kc + 128],
                        Al[:, A_XA(br) + 128 * kc:A_XA(br) + 128 * kc + 128],
                        Ah[:, A_LW(br, kc) + 512 * nch:
                           A_LW(br, kc) + 512 * nch + 512],
                        Al[:, A_LW(br, kc) + 512 * nch:
                           A_LW(br, kc) + 512 * nch + 512],
                        first=(kc == 0), last_flag=(kc == 1), deps=deps,
                        deps_ll=[xal] if kc == 0 else ())
                wb = col(Cf, C_SC(br) + 512 * nch, 512)
                ev = bld.op("scalar",
                            lambda h, s2=s2, wb=wb: h.copy(wb, s2["t"][:, 0:512]),
                            deps=[last2, pst["w_war"][br], pst["expp"][br]]
                            + (cf_ready if t == 0 else []))
                pool.release(s2, ev)
                a["scev"][(br, nch)] = ev
                pst["xa_war"][br] = last2
        rr = {}
        for br in range(2):
            sadd = None
            sadd0 = None
            for nch in range(2):
                wb = col(Cf, C_SC(br) + 512 * nch, 512)
                sadd = bld.op("gpsimd",
                              lambda h, wb=wb, br=br, nch=nch: h.tensor_add(
                                  wb, wb,
                                  Cf[:, C_LB(br) + 512 * nch:
                                     C_LB(br) + 512 * nch + 512]),
                              deps=[a["scev"][(br, nch)], lb_d[br],
                                    pst["w_war"][br]])
                if nch == 0:
                    sadd0 = sadd
            mxc = col(E, E_LC + 10, 1)
            smc = col(E, E_LC + 11, 1)
            rmax = bld.op("vector",
                          lambda h, br=br, mxc=mxc: h.reduce_max(
                              out=mxc, in_=col(Cf, C_SC(br), 1024), axis=AX.X,
                              negate=True),
                          deps=[sadd, sadd0])
            ex = bld.op("scalar",
                        lambda h, br=br, mxc=mxc, smc=smc: h.activation(
                            out=col(Cf, C_PU(br), 1024),
                            in_=col(Cf, C_SC(br), 1024),
                            func=AF.Exp, bias=mxc, scale=1.0, accum_out=smc),
                        deps=[rmax, pst["pu_war"][br]]
                        + (cf_ready + [wgt_st] if t == 0 else []))
            rcol = col(E, E_ZC + 3 + br, 1)
            r = bld.op("vector",
                       lambda h, rcol=rcol, smc=smc: h.reciprocal(rcol, smc),
                       deps=[ex])
            rr[br] = r
            pst["expp"][br] = ex
            pst["sc_war"][br] = rmax
        a["rr"] = rr
        return a

    def stage_av(a):
        t = a["t"]
        rr = a["rr"]
        m8o = col(E, E_BS, 8)
        m8i = col(E, E_LC, 8)
        srco, srci = col(Cf, C_PU(0), 1024), col(Cf, C_PU(1), 1024)
        mro, mri = pst["expp"][0], pst["expp"][1]
        for rnd in range(4):
            dsto = col(Cf, C_WO(rnd % 2), 1024)
            dsti = col(Cf, C_WI(rnd % 2), 1024)
            mxo = bld.op("vector",
                         lambda h, m8o=m8o, srco=srco: h.max(out=m8o, in_=srco),
                         deps=[mro] + (cf_ready if (t == 0 and rnd == 0) else []))
            mxi = bld.op("vector",
                         lambda h, m8i=m8i, srci=srci: h.max(out=m8i, in_=srci),
                         deps=[mri])
            d_o = [mxo]
            d_i = [mxi]
            if rnd == 0:
                d_o.append(pst["expp"][0])
                d_i.append(pst["expp"][1])
            mro = bld.op("vector",
                         lambda h, dsto=dsto, m8o=m8o, srco=srco: h.match_replace(
                             out=dsto, in_to_replace=m8o, in_values=srco,
                             imm_value=0),
                         deps=d_o)
            mri = bld.op("vector",
                         lambda h, dsti=dsti, m8i=m8i, srci=srci: h.match_replace(
                             out=dsti, in_to_replace=m8i, in_values=srci,
                             imm_value=0),
                         deps=d_i)
            srco, srci = dsto, dsti
        pst["w_war"][0] = mro
        pst["w_war"][1] = mri
        muo = bld.op("vector",
                     lambda h, srco=srco: h.tensor_sub(
                         col(Cf, C_PU(0), 1024), col(Cf, C_PU(0), 1024), srco),
                     deps=[mro])
        mui = bld.op("vector",
                     lambda h, srci=srci: h.tensor_sub(
                         col(Cf, C_PU(1), 1024), col(Cf, C_PU(1), 1024), srci),
                     deps=[mri])
        ricol = col(E, E_ZC + 4, 1)
        a["mi"] = bld.op("vector",
                         lambda h, ricol=ricol, t=t: h.tensor_scalar_mul(
                             col(Df, D_MI(t), 1024), col(Cf, C_PU(1), 1024),
                             ricol),
                         deps=[mui, rr[1], pst["mi_war"]]
                         + ([x1_writes[14], x1_writes[15]] if t == 0 else []))
        pst["pu_war"][1] = a["mi"]
        tpre = bld.op("vector",
                      lambda h, t=t: h.tensor_add(
                          col(Df, D_X1(t), 1024), col(Df, D_X1(t), 1024),
                          col(Df, D_BIAS, 1024)),
                      deps=[a["x1r"], bb_d])
        x1_last = tpre
        rcol = col(E, E_ZC + 3, 1)
        obtmp = col(Df, D_OBT, 512)
        for nch in range(2):
            fbc = a["fb"][nch]
            obt = bld.op("vector",
                         lambda h, t=t, nch=nch, rcol=rcol, obtmp=obtmp:
                         h.scalar_tensor_tensor(
                             out=obtmp, in0=col(E, E_FB(t, nch), 512),
                             scalar=rcol,
                             in1=Cf[:, C_PU(0) + 512 * nch:
                                    C_PU(0) + 512 * nch + 512],
                             op0=ALU.mult, op1=ALU.mult),
                         deps=[fbc, rr[0], muo, pst["obt_war"]])
            oba = bld.op("vector",
                         lambda h, t=t, nch=nch, obtmp=obtmp: h.tensor_add(
                             Df[:, D_X1(t) + 512 * nch:D_X1(t) + 512 * nch + 512],
                             Df[:, D_X1(t) + 512 * nch:D_X1(t) + 512 * nch + 512],
                             obtmp),
                         deps=[obt, x1_last])
            pst["fb_war"][(t % 2, nch)] = obt
            pst["obt_war"] = oba
            pst["pu_war"][0] = obt
            x1_last = oba
        a["x1_last"] = x1_last
        return a

    def stage_b1(a):
        t = a["t"]
        mi = a["mi"]
        xg_los = []
        for b in range(2):
            s = pool.acquire()
            last = None
            for j in range(4):
                c = 4 * b + j
                last = bld.op(
                    "tensor",
                    lambda h, s=s, j=j, c=c, t=t: h.matmul(
                        s["t"][:, 128 * j:128 * (j + 1)],
                        Df[:, D_MI(t) + 128 * c:D_MI(t) + 128 * (c + 1)], ID,
                        is_transpose=True, start=(j == 0), stop=(j == 3)),
                    deps=([mi, s["free"], id_d] if j == 0 else []))
            in1h = Bh[:, 0:16384].rearrange("p (c n) -> p c n", c=8)[
                :, 4 * b:4 * b + 4, 128 * t:128 * t + 128]
            in1l = Bl[:, 0:16384].rearrange("p (c n) -> p c n", c=8)[
                :, 4 * b:4 * b + 4, 128 * t:128 * t + 128]
            i3 = s["t"][:, 0:512].rearrange("p (c n) -> p c n", c=4)
            g1 = bld.op("vector",
                        lambda h, b=b, i3=i3, in1h=in1h: h.scalar_tensor_tensor(
                            out=Df[:, D_XG32(b):D_XG32(b) + 512], in0=i3,
                            scalar=0.0, in1=in1h, op0=ALU.bypass, op1=ALU.mult),
                        deps=[last, x2_los[t][b]]
                        + ([qt_lastmm, p2_lastS] if (t == 0 and b == 0) else []))
            g2 = bld.op("vector",
                        lambda h, i3=i3, in1l=in1l: h.scalar_tensor_tensor(
                            out=col(Df, D_TMP2, 512), in0=i3,
                            scalar=0.0, in1=in1l, op0=ALU.bypass, op1=ALU.mult),
                        deps=[last, pst["tmp2_war"]]
                        + ([qt_lastmm, p2_lastS] if (t == 0 and b == 0) else []))
            g3 = bld.op("vector",
                        lambda h, b=b: h.tensor_add(
                            Df[:, D_XG32(b):D_XG32(b) + 512],
                            Df[:, D_XG32(b):D_XG32(b) + 512],
                            col(Df, D_TMP2, 512)),
                        deps=[g1, g2])
            pst["tmp2_war"] = g3
            pool.release(s, g3)
            xgh, xgl = emit_split(col(Ah, A_XG(b), 512), col(Al, A_XG(b), 512),
                                  Df[:, D_XG32(b):D_XG32(b) + 512],
                                  [g3, pst["xgt_war"]], lo_eng="gpsimd")
            xg_los.append((xgh, xgl))
        a["xg_los"] = xg_los
        return a

    def stage_b2(a, a2=None):
        t = a["t"]
        xg_los = a["xg_los"]
        ib_last = None
        last = None
        for nch in range(2):
            s = pool.acquire()
            for kc in range(8):
                deps = [s["free"], xg_los[kc // 4][0]] if kc == 0 else [
                    xg_los[kc // 4][0]]
                last = mm3(
                    s["t"][:, 0:512],
                    Ah[:, A_XG(0) + 128 * kc:A_XG(0) + 128 * kc + 128],
                    Al[:, A_XG(0) + 128 * kc:A_XG(0) + 128 * kc + 128],
                    Ah[:, A_W(kc) + 512 * nch:A_W(kc) + 512 * nch + 512],
                    Al[:, A_W(kc) + 512 * nch:A_W(kc) + 512 * nch + 512],
                    first=(kc == 0), last_flag=(kc == 7), deps=deps,
                    deps_ll=[xg_los[kc // 4][1]])
            ibe = bld.op("vector",
                         lambda h, s=s, nch=nch, t=t: h.tensor_add(
                             Df[:, D_X1(t) + 512 * nch:D_X1(t) + 512 * nch + 512],
                             Df[:, D_X1(t) + 512 * nch:D_X1(t) + 512 * nch + 512],
                             s["t"][:, 0:512]),
                         deps=[last, a["x1_last"]])
            pool.release(s, ibe)
            ib_last = ibe
        pst["xgt_war"] = last
        pst["mi_war"] = last
        odma = bld.dma("sync", "OUT", outd[128 * t:128 * (t + 1), :],
                       col(Df, D_X1(t), 1024), deps=[ib_last])
        out_dmas.append(odma)
        pst["x1_readers"][t % 2] = odma
        if a2 is not None:
            t2 = a2["t"]
            a2["x1r"] = bld.dma("gpsimd", "XR%d" % (t2 % 2),
                                col(Df, D_X1(t2), 1024),
                                x1d[128 * t2:128 * (t2 + 1), :],
                                deps=[x1_writes[t2], odma])
        if stop == "p3t0" and t == 0:
            _dump_and_finish(nc, bld, [("Cf", Cf, 0, 8192), ("Df", Df, 0, 8192),
                                       ("E", E, 0, E_COLS)])
            return False
        return True

    aa = [None] * (NTT + 2)
    aa[0] = stage_amm(0)
    stage_amm2a(aa[0])
    stage_amm2b(aa[0])
    stage_av(aa[0])
    aa[1] = stage_amm(1)
    stage_amm2a(aa[1])
    stage_amm2b(aa[1])
    for t in range(NTT):
        if t + 2 < NTT:
            aa[t + 2] = stage_amm(t + 2)
        stage_b1(aa[t])
        if t + 2 < NTT:
            stage_amm2a(aa[t + 2])
        if t + 1 < NTT:
            stage_av(aa[t + 1])
        if not stage_b2(aa[t], aa[t + 2] if t + 2 < NTT else None):
            return nc
        if t + 2 < NTT:
            stage_amm2b(aa[t + 2])

    bld.op("sync", lambda h: h.nop(), deps=[out_dmas[-1]])
    _realize(nc, bld)
    return nc


def _prep_shared(inp):
    f32 = np.float32
    omega = np.asarray(inp["L0_attn_omega"], f32)
    om2v = np.ascontiguousarray(np.concatenate([omega.T, omega.T], axis=0))
    wc = np.zeros((DM, 64), f32)
    for br, key in ((0, "L0_ffn_so_conv_w"), (1, "L0_ffn_si_conv_w")):
        cw = np.asarray(inp[key], f32)
        for c in range(8):
            for r in range(128):
                wc[128 * c + r, 32 * br + r // 4] = cw[32 * c + r // 4, r % 4]
    s16 = np.zeros((128, 128), f32)
    for c in range(8):
        for p in range(128):
            s16[p, 16 * c + 2 * c + p // 64] = 1.0
    vecs = np.zeros((128, 12), f32)
    for br, pre in ((0, "so"), (1, "si")):
        for j, key in enumerate(("conv_b", "gamma", "beta")):
            v = np.asarray(inp[f"L0_ffn_{pre}_{key}"], f32).reshape(2, 128).T
            vecs[:, 6 * br + 2 * j:6 * br + 2 * j + 2] = v
    return {
        "wq": np.ascontiguousarray(np.asarray(inp["L0_attn_Wq"], f32)),
        "wk": np.ascontiguousarray(np.asarray(inp["L0_attn_Wk"], f32)),
        "wv": np.ascontiguousarray(np.asarray(inp["L0_attn_Wv"], f32)),
        "wgt": np.ascontiguousarray(np.asarray(inp["L0_ffn_weight"], f32)),
        "om2": om2v,
        "lwso": np.ascontiguousarray(np.asarray(inp["L0_ffn_so_lin_w"], f32).T),
        "lwsi": np.ascontiguousarray(np.asarray(inp["L0_ffn_si_lin_w"], f32).T),
        "lbso": np.ascontiguousarray(np.broadcast_to(
            np.asarray(inp["L0_ffn_so_lin_b"], f32), (128, DM))),
        "lbsi": np.ascontiguousarray(np.broadcast_to(
            np.asarray(inp["L0_ffn_si_lin_b"], f32), (128, DM))),
        "bbc": np.ascontiguousarray(np.broadcast_to(
            np.asarray(inp["L0_ffn_bias"], f32), (128, DM))),
        "wc": wc,
        "vecs": vecs,
        "s16": s16,
        "ident": np.eye(128, dtype=f32),
        "ones": np.ones((128, 1), f32),
        "csts": np.ascontiguousarray(np.broadcast_to(
            np.array([1e-5, -LN16], f32), (128, 2))),
    }


def kernel(**inputs):
    x = np.ascontiguousarray(np.asarray(inputs["L0_attn_x"], np.float32))
    shared = _prep_shared(inputs)
    nc = build_program()
    in_maps = []
    for i in range(NCORES):
        m = dict(shared)
        m["x"] = np.ascontiguousarray(x[i])
        in_maps.append(m)
    res = run_bass_kernel_spmd(nc, in_maps, core_ids=list(range(NCORES)))
    out = np.stack([np.asarray(res.results[i]["out"]) for i in range(NCORES)],
                   axis=0)
    return out
